# version 22
# baseline (speedup 1.0000x reference)
import numpy as np
import concourse.bass as bass
import concourse.mybir as mybir
from concourse.bass_utils import run_bass_kernel_spmd

F32 = mybir.dt.float32
F16 = mybir.dt.float16
AF = mybir.ActivationFunctionType
OP = mybir.AluOpType

P = 128
T = 4096
W = 512
NWIN = T // W
DM = 1024
DI = 2048
NST = 16
DTR = 64
MT = DI // P
DT = DM // P

DEBUG = False
PARANOID = False
import os as _os
PHASES = _os.environ.get("KPHASES", "ASC")

ENGINES = ("sync", "tensor", "scalar", "vector", "gpsimd")


class Rec:

    def __init__(self):
        self.ops = []
        self.last_w = {}
        self.readers = {}

    def op(self, eng, fn, reads=(), writes=(), dma=False, tag=""):
        i = len(self.ops)
        deps = set()
        for b in reads:
            if b in self.last_w:
                deps.add(self.last_w[b])
        for b in writes:
            if b in self.last_w:
                deps.add(self.last_w[b])
            for r in self.readers.get(b, ()):
                deps.add(r)
        self.ops.append(dict(eng=eng, fn=fn, dma=dma, deps=deps, tag=tag,
                             flagged=False, sem=None, val=None, gate=None))
        for b in reads:
            self.readers.setdefault(b, []).append(i)
        for b in writes:
            self.last_w[b] = i
            self.readers[b] = []
        return i

    def schedule(self):
        ops = self.ops
        for o in ops:
            kept = set()
            for d in o["deps"]:
                p = ops[d]
                if p["dma"] or p["eng"] != o["eng"] or PARANOID:
                    kept.add(d)
            o["deps"] = kept
            for d in kept:
                ops[d]["flagged"] = True
        CAP = 4000
        NPOOL = 48
        cc = {e: 0 for e in ENGINES}
        dseq = {e: 0 for e in ENGINES}
        uses = {}
        self.sem_names = set()
        for o in ops:
            if o["dma"]:
                o["flagged"] = True
                q = o["eng"]
                pi = dseq[q] % NPOOL
                dseq[q] += 1
                key = (q, pi)
                uses[key] = uses.get(key, 0) + 1
                o["sem"], o["val"] = f"q_{q}_{pi}", 16 * uses[key]
                o["gate"] = (o["sem"], o["val"] - 16) if uses[key] > 1 else None
            elif o["flagged"]:
                v = cc[o["eng"]] + 1
                cc[o["eng"]] += 1
                ep, vin = divmod(v - 1, CAP)
                o["sem"], o["val"] = f"c_{o['eng']}{ep}", vin + 1
            if o["sem"]:
                self.sem_names.add(o["sem"])

    def emit_engine(self, eng_name, eng, sems, waited):
        wt = waited[eng_name]
        for o in self.ops:
            if o["eng"] != eng_name:
                continue
            tgt = {}
            if o.get("gate"):
                tgt[o["gate"][0]] = o["gate"][1]
            for d in o["deps"]:
                p = self.ops[d]
                if p["sem"] is None:
                    continue
                if (not p["dma"]) and p["eng"] == eng_name and not PARANOID:
                    continue
                if tgt.get(p["sem"], 0) < p["val"]:
                    tgt[p["sem"]] = p["val"]
            for s, v in sorted(tgt.items()):
                if wt.get(s, 0) < v:
                    eng.wait_ge(sems[s], v)
                    wt[s] = v
            ins = o["fn"](eng)
            if o["flagged"]:
                assert ins is not None, f"flagged op returned None: {o['tag']}"
                ins.then_inc(sems[o["sem"]], 16 if o["dma"] else 1)


def build_program(debug):
    nc = bass.Bass("TRN2", target_bir_lowering=False)
    d = {}

    def din(name, shape, dt=F32):
        d[name] = nc.dram_tensor(name, list(shape), dt, kind="ExternalInput").ap()
        return d[name]

    x_h = din("x", (DM, T))
    w_in_h = din("w_in", (2 * MT, P, DT * P))
    w_op_h = din("w_op", (DT, P, MT * P))
    w_f1_h = din("w_f1", (DT, P, DM))
    w_f2_h = din("w_f2", (DT, P, DM))
    w_n0_h = din("w_n0", (DT, P, DM))
    w_n1_h = din("w_n1", (DT, P, DM))
    w_n2_h = din("w_n2", (DT, P, DM))
    w_xp_h = din("w_xp", (MT, P, 96))
    w_dt_h = din("w_dt", (DTR, DI))
    vh = {}
    for nm, cols in [("v_nw", DT), ("v_cw", MT * 4), ("v_cb", MT), ("v_dtb", MT),
                     ("v_dp", MT), ("v_alog", MT * NST), ("v_fb1", DT), ("v_fb2", DT),
                     ("v_lw", DT), ("v_lb", DT), ("v_nb", DT)]:
        vh[nm] = din(nm, (P, cols))
    sel_h = din("c_sel", (NST, NST * P), F16)
    ide_h = din("c_ide", (P, P), F16)
    out_h = nc.dram_tensor("out", [DM, T], F32, kind="ExternalOutput").ap()

    scr = {}
    for nm, shape in [("k_in", (2 * MT, P, DT * P)), ("k_op", (DT, P, MT * P)),
                      ("k_f1", (DT, P, DM)), ("k_f2", (DT, P, DM)),
                      ("k_n0", (DT, P, DM)), ("k_n1", (DT, P, DM)),
                      ("k_n2", (DT, P, DM))]:
        scr[nm] = nc.dram_tensor(nm, list(shape), F16, kind="Internal").ap()

    dump_h = {}
    if debug:
        for nm, shape, dt in [("xn", (P, DT * W), F16), ("uc", (P, MT * W), F16),
                              ("res", (P, MT * W), F16), ("delta", (P, MT * W), F16),
                              ("dtr", (DTR, W), F16), ("brow", (NST, 2 * W), F16),
                              ("bbc", (P, 2 * W), F16), ("hh", (P, W), F16),
                              ("y3", (P, MT * W), F16), ("o16", (P, DT * W), F16),
                              ("zz", (P, DT * W), F16), ("xmid", (P, DT * W), F16),
                              ("ym0", (P, W), F16), ("da00", (P, W), F32),
                              ("wb00", (P, W), F16), ("hh00", (P, W), F16),
                              ("z2t", (P, DT * W), F16), ("v2", (P, DT * W), F16),
                              ("auc", (NWIN, P, MT * W), F16),
                              ("adelta", (NWIN, P, MT * W), F16),
                              ("ay3", (NWIN, P, MT * W), F16),
                              ("ao16", (NWIN, P, DT * W), F16),
                              ("azz", (NWIN, P, DT * W), F16),
                              ("axv", (NWIN, P, DT * W), F16),
                              ("ax", (NWIN, P, DT * W), F16),
                              ("arstd", (NWIN, P, W), F16),
                              ("axn", (NWIN, P, DT * W), F16),
                              ("tl", (NWIN, P, MT * 3), F16),
                              ("ue0", (NWIN, P, W + 3), F16)]:
            dump_h[nm] = nc.dram_tensor("dump_" + nm, list(shape), dt,
                                        kind="ExternalOutput").ap()

    from contextlib import ExitStack
    ctx = ExitStack()

    def sb(name, shape, dt):
        return ctx.enter_context(nc.sbuf_tensor(name, list(shape), dt))

    def psum(name, shape):
        return ctx.enter_context(nc.psum_tensor(name, list(shape), F32))

    with ctx:
        b_xp = sb("b_xp", (P, MT * 96), F16)
        b_dt = sb("b_dt", (DTR, DI), F16)
        bv = {nm: sb("b" + nm, (P, c), F32) for nm, c in
              [("v_nw", DT), ("v_cw", MT * 4), ("v_cb", MT), ("v_dtb", MT),
               ("v_dp", MT), ("v_alog", MT * NST), ("v_fb1", DT), ("v_fb2", DT),
               ("v_lw", DT), ("v_lb", DT), ("v_nb", DT)]}
        b_aneg = sb("b_aneg", (P, MT * NST), F32)
        b_sel = sb("b_sel", (NST, NST * P), F16)
        b_ide = sb("b_ide", (P, P), F16)
        b_ones = sb("b_ones", (P, P), F16)
        b_cst = sb("b_cst", (P, 2), F32)
        b_win = [sb(f"b_win{i}", (P, DT * P), F16) for i in range(2)]
        b_wop = [sb(f"b_wop{i}", (P, MT * P), F16) for i in range(2)]
        b_wsm = [sb(f"b_wsm{i}", (P, DM), F16) for i in range(2)]
        b_x = [sb(f"b_x{i}", (P, DT * W), F16) for i in range(2)]
        b_xstg = [sb(f"b_xstg{i}", (P, W), F32) for i in range(2)]
        b_pstg = [sb(f"b_pstg{i}", (P, W), F32) for i in range(2)]
        b_pstg16 = [sb(f"b_pstg16{i}", (P, W), F16) for i in range(2)]
        b_xn = sb("b_xn", (P, DT * W), F16)
        b_x2 = [sb("b_x20", (P, W), F16)] * 2
        b_rstd = sb("b_rstd", (P, W), F16)
        b_ue = [sb(f"b_ue{i}", (P, W + 3), F16) for i in range(2)]
        b_tail = sb("b_tail", (P, MT * 3), F16)
        b_uc = sb("b_uc", (P, MT * W), F16)
        b_res = sb("b_res", (P, MT * W), F16)
        b_delta = sb("b_delta", (P, MT * W), F16)
        b_dtr = sb("b_dtr", (DTR, W), F16)
        b_brow = [sb("b_brow0", (NST, 2 * W), F16)] * 2
        b_vc = [sb("b_vc0", (P, W), F32)] * 2
        b_e32 = [sb("b_e320", (P, W), F32)] * 2
        b_cv = [sb("b_cv0", (P, W), F16)] * 2
        b_ct = [sb("b_ct0", (P, W), F16)] * 2
        b_bbc = sb("b_bbc", (P, NST * 2 * W), F16)
        b_da = [sb(f"b_da{i}", (P, W), F32) for i in range(2)]
        b_wb = [sb(f"b_wb{i}", (P, W), F16) for i in range(2)]
        b_hh = [sb(f"b_hh{i}", (P, W), F16) for i in range(2)]
        b_tmp = [sb(f"b_tmp{i}", (P, W), F16) for i in range(2)]
        b_wm = [sb(f"b_wm{i}", (P, W), F16) for i in range(2)]
        b_g = sb("b_g", (P, W), F16)
        b_hst = sb("b_hst", (P, MT * NST), F32)
        b_o16 = sb("b_o16", (P, DT * W), F16)
        b_p1 = sb("b_p1", (P, DT * W), F16)
        b_p2 = sb("b_p2", (P, DT * W), F16)
        b_mu = sb("b_mu", (P, W), F16)
        b_var = sb("b_var", (P, W), F32)
        b_cln = sb("b_cln", (P, W), F32)
        b_rstdc = sb("b_rstdc", (P, W), F16)
        b_z2 = [sb(f"b_z2{i}", (P, W), F16) for i in range(2)]
        b_q = [sb(f"b_q{i}", (P, W), F16) for i in range(2)]
        b_o32 = [sb("b_o320", (P, W), F32)] * 2

        ps_in = [psum("ps_in0", (P, W)), psum("ps_in1", (P, W))]
        ps_xp1 = psum("ps_xp1", (P, W))
        ps_bc = [psum("ps_bc0", (P, W)), psum("ps_bc1", (P, W))]
        ps_c = [psum("ps_c0", (P, W)), psum("ps_c1", (P, W))]

        R = Rec()
        vo = lambda fn, r, w_, tg="": R.op("vector", fn, r, w_, tag=tg)
        so = lambda fn, r, w_, tg="": R.op("scalar", fn, r, w_, tag=tg)
        to = lambda fn, r, w_, tg="": R.op("tensor", fn, r, w_, tag=tg)
        go = lambda fn, r, w_, tg="": R.op("gpsimd", fn, r, w_, tag=tg)
        dm = lambda eng, fn, r, w_, tg="": R.op(eng, fn, r, w_, dma=True, tag=tg)

        def ws(buf, j, width=W):
            return buf[:, j * width:(j + 1) * width]

        for nm in bv:
            dm("sync", lambda e, o=bv[nm], i=vh[nm]: e.dma_start(out=o[:], in_=i[:]),
               [nm], ["sb_" + nm], tg="ld" + nm)
        dm("sync", lambda e: e.dma_start(out=b_sel[:], in_=sel_h[:]),
           ["c_sel"], ["b_sel"], tg="ldsel")
        dm("sync", lambda e: e.dma_start(out=b_ide[:], in_=ide_h[:]),
           ["c_ide"], ["b_ide"], tg="ldide")
        for k in range(MT):
            dm("sync", lambda e, kk=k: e.dma_start(
                out=b_xstg[0][:, 0:96], in_=w_xp_h[kk]),
               ["w_xp"], ["xstg0"], tg=f"ldxp{k}")
            vo(lambda e, kk=k: e.tensor_copy(
                out=b_xp[:, kk * 96:(kk + 1) * 96], in_=b_xstg[0][:, 0:96]),
               ["xstg0"], ["b_xp"], f"cxp{k}")
        for k in range(4):
            dm("sync", lambda e, kk=k: e.dma_start(
                out=b_xstg[1][0:DTR, :], in_=w_dt_h[:, kk * W:(kk + 1) * W]),
               ["w_dt"], ["xstg1"], tg=f"lddt{k}")
            vo(lambda e, kk=k: e.tensor_copy(
                out=b_dt[:, kk * W:(kk + 1) * W], in_=b_xstg[1][0:DTR, :]),
               ["xstg1"], ["b_dt"], f"cdt{k}")
        so(lambda e: e.activation(out=b_aneg[:], in_=bv["v_alog"][:], func=AF.Exp),
           ["sb_v_alog"], ["b_aneg"], "aexp")
        vo(lambda e: e.tensor_scalar(out=b_aneg[:], in0=b_aneg[:], scalar1=-1.0,
                                     scalar2=None, op0=OP.mult),
           ["b_aneg"], ["b_aneg"], "aneg")
        vo(lambda e: e.memset(b_ones[:], 1.0), [], ["b_ones"], "ones")
        vo(lambda e: e.memset(b_cst[:, 0:1], 1e-5), [], ["b_cst"], "cst0")
        vo(lambda e: e.memset(b_cst[:, 1:2], 1.0), [], ["b_cst"], "cst1")
        vo(lambda e: e.memset(b_hst[:], 0.0), [], [f"hst{i}" for i in range(MT * NST)], "hst0")
        vo(lambda e: e.memset(b_tail[:], 0.0), [], ["b_tail"], "tail0")

        for src, dst, key in [(w_in_h, scr["k_in"], "k_in"), (w_op_h, scr["k_op"], "k_op"),
                              (w_f1_h, scr["k_f1"], "k_f1"), (w_f2_h, scr["k_f2"], "k_f2"),
                              (w_n0_h, scr["k_n0"], "k_n0"), (w_n1_h, scr["k_n1"], "k_n1"),
                              (w_n2_h, scr["k_n2"], "k_n2")]:
            cols = src.shape[2]
            nh = cols // W
            for bi in range(src.shape[0]):
                for hi in range(nh):
                    j = (bi * nh + hi) % 2
                    stg = b_pstg[j]
                    f16 = b_pstg16[j]
                    dm("sync", lambda e, o=stg, i=src[bi][:, hi * W:(hi + 1) * W]:
                       e.dma_start(out=o[:], in_=i),
                       [key[2:]], [f"pstg{j}"], tg=f"prep{key}{bi}_{hi}l")
                    vo(lambda e, o=f16, i=stg: e.tensor_copy(out=o[:], in_=i[:]),
                       [f"pstg{j}"], [f"pstg16{j}"], f"prep{key}{bi}_{hi}c")
                    dm("sync", lambda e, o=dst[bi][:, hi * W:(hi + 1) * W], i=f16:
                       e.dma_start(out=o, in_=i[:]),
                       [f"pstg16{j}"], [f"{key}_{bi}"], tg=f"prep{key}{bi}_{hi}s")

        def layer_norm(src_buf, skey, dst_buf, dkey, gamma_key, w, tg, out32_t0=None):
            for k in range(DT):
                to(lambda e, s=src_buf, kk=k, st=(k == 0), sp=(k == DT - 1):
                   e.matmul(out=ps_c[0][:], lhsT=b_ones[:], rhs=ws(s, kk),
                            start=st, stop=sp),
                   ["b_ones", f"{skey}_{k}"], ["psc0"], f"{tg}s1_{k}")
            for k in range(DT):
                z2 = b_z2[k % 2]
                vo(lambda e, o=z2, s=src_buf, kk=k: e.tensor_mul(
                    out=o[:], in0=ws(s, kk), in1=ws(s, kk)),
                   [f"{skey}_{k}"], [f"z2_{k % 2}"], f"{tg}sq{k}")
                to(lambda e, z=z2, st=(k == 0), sp=(k == DT - 1):
                   e.matmul(out=ps_c[1][:], lhsT=b_ones[:], rhs=z[:], start=st, stop=sp),
                   ["b_ones", f"z2_{k % 2}"], ["psc1"], f"{tg}s2_{k}")
            so(lambda e: e.mul(out=b_mu[:], in_=ps_c[0][:], mul=1.0 / DM),
               ["psc0"], ["b_mu"], f"{tg}mu")
            vo(lambda e: e.tensor_mul(out=b_z2[0][:], in0=b_mu[:], in1=b_mu[:]),
               ["b_mu"], ["z2_0"], f"{tg}mus")
            vo(lambda e: e.scalar_tensor_tensor(
                out=b_var[:], in0=ps_c[1][:], scalar=1.0 / DM, in1=b_z2[0][:],
                op0=OP.mult, op1=OP.subtract),
               ["psc1", "z2_0"], ["b_var"], f"{tg}var")
            so(lambda e: e.activation(out=b_cln[:], in_=b_var[:], func=AF.Ln,
                                      bias=b_cst[:, 0:1]),
               ["b_var", "b_cst"], ["b_cln"], f"{tg}ln")
            so(lambda e: e.activation(out=b_rstdc[:], in_=b_cln[:], func=AF.Exp,
                                      scale=-0.5),
               ["b_cln"], ["b_rstdc"], f"{tg}ex")
            for k in range(DT):
                q = b_q[k % 2]
                vo(lambda e, o=q, s=src_buf, kk=k: e.tensor_sub(
                    out=o[:], in0=ws(s, kk), in1=b_mu[:]),
                   [f"{skey}_{k}", "b_mu"], [f"q_{k % 2}"], f"{tg}q{k}")
                if out32_t0 is not None:
                    o32 = b_o32[k % 2]
                    vo(lambda e, o=o32, i=q: e.tensor_mul(out=o[:], in0=i[:],
                                                          in1=b_rstdc[:]),
                       [f"q_{k % 2}", "b_rstdc"], ["o32"], f"{tg}o{k}")
                    dm("sync", lambda e, i=o32, kk=k, t0=out32_t0: e.dma_start(
                        out=out_h[kk * P:(kk + 1) * P, t0:t0 + W], in_=i[:]),
                       ["o32"], ["out"], f"{tg}st{k}")
                elif gamma_key is not None:
                    vo(lambda e, o=q, i=q: e.tensor_mul(out=o[:], in0=i[:],
                                                        in1=b_rstdc[:]),
                       [f"q_{k % 2}", "b_rstdc"], [f"q_{k % 2}"], f"{tg}qr{k}")
                    vo(lambda e, o_=dst_buf, i=q, kk=k: e.tensor_scalar(
                        out=ws(o_, kk), in0=i[:], scalar1=bv["v_lw"][:, kk:kk + 1],
                        scalar2=bv["v_lb"][:, kk:kk + 1], op0=OP.mult, op1=OP.add),
                       [f"q_{k % 2}", "sb_v_lw", "sb_v_lb"], [f"{dkey}_{k}"],
                       f"{tg}ap{k}")
                else:
                    vo(lambda e, o_=dst_buf, i=q, kk=k: e.tensor_mul(
                        out=ws(o_, kk), in0=i[:], in1=b_rstdc[:]),
                       [f"q_{k % 2}", "b_rstdc"], [f"{dkey}_{k}"], f"{tg}qr{k}")

        pending_c = []

        def drain_c(k=None):
            nonlocal_marker = None
            n = len(pending_c) if k is None else min(k, len(pending_c))
            for _ in range(n):
                pending_c.pop(0)()

        for w in range(NWIN):
            xb = b_x[w % 2]
            xkey = f"x{w % 2}"
            t0 = w * W
            wpar = w % 2

            for k in range(DT):
                stg = b_xstg[k % 2]
                dm("sync", lambda e, o=stg, i=x_h[k * P:(k + 1) * P, t0:t0 + W]:
                   e.dma_start(out=o[:], in_=i),
                   ["x"], [f"xstg{k % 2}"], f"w{w}ldx{k}")
                vo(lambda e, o=ws(xb, k), i=stg: e.tensor_copy(out=o, in_=i[:]),
                   [f"xstg{k % 2}"], [f"{xkey}_{k}"], f"w{w}cx{k}")
            for k in range(DT):
                x2 = b_x2[k % 2]
                vo(lambda e, o=x2, i=ws(xb, k): e.tensor_mul(out=o[:], in0=i, in1=i),
                   [f"{xkey}_{k}"], ["x2_0"], f"w{w}x2_{k}")
                to(lambda e, i=x2, st=(k == 0), sp=(k == DT - 1):
                   e.matmul(out=ps_xp1[:], lhsT=b_ones[:], rhs=i[:], start=st, stop=sp),
                   ["b_ones", "x2_0"], ["ps_xp1"], f"w{w}xs{k}")
            so(lambda e: e.activation(out=b_vc[0][:], in_=ps_xp1[:], func=AF.Ln,
                                      scale=1.0 / DM, bias=b_cst[:, 0:1]),
               ["ps_xp1", "b_cst"], ["vc0"], f"w{w}rln")
            so(lambda e: e.activation(out=b_rstd[:], in_=b_vc[0][:], func=AF.Exp,
                                      scale=-0.5),
               ["vc0"], ["b_rstd"], f"w{w}rex")
            for k in range(DT):
                vo(lambda e, o=ws(b_xn, k), i=ws(xb, k), kk=k:
                   e.scalar_tensor_tensor(out=o, in0=i, scalar=bv["v_nw"][:, kk:kk + 1],
                                          in1=b_rstd[:], op0=OP.mult, op1=OP.mult),
                   [f"{xkey}_{k}", "b_rstd", "sb_v_nw"], [f"xn_{k}"], f"w{w}xn{k}")

            if debug:
                dm("sync", lambda e, ww=w, xx=xb: e.dma_start(out=dump_h["ax"][ww], in_=xx[:]),
                   [f"{xkey}_{k}" for k in range(DT)], ["dumped_ax"], f"d_ax{w}")
                dm("sync", lambda e, ww=w: e.dma_start(out=dump_h["arstd"][ww], in_=b_rstd[:]),
                   ["b_rstd"], ["dumped_arstd"], f"d_ars{w}")
                dm("sync", lambda e, ww=w: e.dma_start(out=dump_h["axn"][ww], in_=b_xn[:]),
                   [f"xn_{k}" for k in range(DT)], ["dumped_axn"], f"d_axn{w}")
            for m in range(2 * MT):
                wbuf = b_win[m % 2]
                dm("sync", lambda e, o=wbuf, i=scr["k_in"][m]: e.dma_start(out=o[:], in_=i),
                   [f"k_in_{m}"], [f"win{m % 2}"], f"w{w}win{m}")
                pb = ps_in[m % 2]
                pkey = f"psin{m % 2}"
                for k in range(DT):
                    to(lambda e, o=pb, l=wbuf, kk=k, st=(k == 0), sp=(k == DT - 1):
                       e.matmul(out=o[:], lhsT=l[:, kk * P:(kk + 1) * P],
                                rhs=ws(b_xn, kk), start=st, stop=sp),
                       [f"win{m % 2}", f"xn_{k}"], [pkey], f"w{w}ip{m}_{k}")
                if m < MT:
                    ue = b_ue[m % 2]
                    uek = f"ue{m % 2}"
                    vo(lambda e, o=ue, mm=m: e.tensor_copy(
                        out=o[:, 0:3], in_=b_tail[:, mm * 3:mm * 3 + 3]),
                       ["b_tail"], [uek], f"w{w}uh{m}")
                    vo(lambda e, o=ue, pp=pb: e.tensor_copy(out=o[:, 3:3 + W], in_=pp[:]),
                       [pkey], [uek], f"w{w}ue{m}")
                    go(lambda e, i=ue, mm=m: e.tensor_copy(
                        out=b_tail[:, mm * 3:mm * 3 + 3], in_=i[:, W:W + 3]),
                       [uek], ["b_tail"], f"w{w}ut{m}")
                    if debug and m == 0:
                        dm("gpsimd", lambda e, ww=w, i=ue: e.dma_start(
                            out=dump_h["ue0"][ww], in_=i[:]),
                           [uek], ["dumped_ue0"], f"d_ue{w}")
                    cv = b_cv[m % 2]
                    vo(lambda e, o=cv, i=ue, mm=m: e.tensor_scalar(
                        out=o[:], in0=i[:, 0:W], scalar1=bv["v_cw"][:, mm * 4:mm * 4 + 1],
                        scalar2=bv["v_cb"][:, mm:mm + 1], op0=OP.mult, op1=OP.add),
                       [uek, "sb_v_cw", "sb_v_cb"], ["cv0"], f"w{w}c0_{m}")
                    for kk in range(1, 4):
                        ct = b_ct[m % 2]
                        vo(lambda e, o=ct, i=ue, mm=m, k2=kk: e.tensor_scalar(
                            out=o[:], in0=i[:, k2:k2 + W],
                            scalar1=bv["v_cw"][:, mm * 4 + k2:mm * 4 + k2 + 1],
                            scalar2=None, op0=OP.mult),
                           [uek, "sb_v_cw"], ["ct0"], f"w{w}c{kk}_{m}")
                        vo(lambda e, o=cv, i=ct: e.tensor_add(out=o[:], in0=o[:],
                                                              in1=i[:]),
                           ["ct0", "cv0"], ["cv0"],
                           f"w{w}ca{kk}_{m}")
                    so(lambda e, o=ws(b_uc, m), i=cv: e.activation(
                        out=o, in_=i[:], func=AF.Silu),
                       ["cv0"], [f"uc_{m}"], f"w{w}silu{m}")
                else:
                    so(lambda e, o=ws(b_res, m - MT), pp=pb: e.activation(
                        out=o, in_=pp[:], func=AF.Silu),
                       [pkey], [f"res_{m - MT}"], f"w{w}rsilu{m - MT}")

            if debug:
                dm("gpsimd", lambda e, ww=w: e.dma_start(out=dump_h["tl"][ww], in_=b_tail[:]),
                   ["b_tail"], ["dumped_tl"], f"d_tl{w}")
            for (c0, c1, pt, pkey) in [(0, 64, ps_xp1, "ps_xp1"),
                                       (64, 80, ps_in[0], "psin0"),
                                       (80, 96, ps_in[1], "psin1")]:
                for k in range(MT):
                    to(lambda e, o=pt, kk=k, a=c0, b=c1, st=(k == 0), sp=(k == MT - 1):
                       e.matmul(out=o[0:b - a, :], lhsT=b_xp[:, kk * 96 + a:kk * 96 + b],
                                rhs=ws(b_uc, kk), start=st, stop=sp),
                       ["b_xp", f"uc_{k}"], [pkey], f"w{w}xp{c0}_{k}")
            vo(lambda e: e.tensor_copy(out=b_dtr[:], in_=ps_xp1[0:DTR, :]),
               ["ps_xp1"], ["b_dtr"], f"w{w}dtr")
            br = b_brow[0]
            brk = "brow0"
            vo(lambda e, o=br: e.tensor_copy(out=o[0:NST, 0:W], in_=ps_in[0][0:NST, :]),
               ["psin0"], [brk], f"w{w}bev")
            vo(lambda e, o=br: e.tensor_copy(out=o[0:NST, W:2 * W], in_=ps_in[1][0:NST, :]),
               ["psin1"], [brk], f"w{w}cev")

            for m in range(MT):
                pb = ps_in[m % 2]
                pkey = f"psin{m % 2}"
                to(lambda e, o=pb, mm=m: e.matmul(
                    out=o[:], lhsT=b_dt[:, mm * P:(mm + 1) * P], rhs=b_dtr[:],
                    start=True, stop=True),
                   ["b_dt", "b_dtr"], [pkey], f"w{w}dt{m}")
                vc = b_vc[m % 2]
                vo(lambda e, o=vc, pp=pb, mm=m: e.tensor_scalar(
                    out=o[:], in0=pp[:], scalar1=bv["v_dtb"][:, mm:mm + 1],
                    scalar2=80.0, op0=OP.add, op1=OP.min),
                   [pkey, "sb_v_dtb"], ["vc0"], f"w{w}sp0_{m}")
                e32 = b_e32[m % 2]
                so(lambda e, o=e32, i=vc: e.activation(out=o[:], in_=i[:], func=AF.Exp),
                   ["vc0"], ["e320"], f"w{w}sp1_{m}")
                so(lambda e, o=ws(b_delta, m), i=e32: e.activation(
                    out=o, in_=i[:], func=AF.Ln, bias=b_cst[:, 1:2]),
                   ["e320", "b_cst"], [f"dl_{m}"], f"w{w}sp2_{m}")

            if debug:
                dm("sync", lambda e, ww=w: e.dma_start(out=dump_h["auc"][ww], in_=b_uc[:]),
                   [f"uc_{k}" for k in range(MT)], ["dumped_auc"], f"d_auc{w}")
                dm("sync", lambda e, ww=w: e.dma_start(out=dump_h["adelta"][ww], in_=b_delta[:]),
                   [f"dl_{k}" for k in range(MT)], ["dumped_adelta"], f"d_adl{w}")
            if "S" not in PHASES:
                continue
            for n in range(NST):
                pb = ps_bc[n % 2]
                pkey = f"psbc{n % 2}"
                to(lambda e, o=pb, nn=n, b=br: e.matmul(
                    out=o[:], lhsT=b_sel[:, nn * P:(nn + 1) * P], rhs=b[0:NST, 0:W],
                    start=True, stop=True),
                   ["b_sel", brk], [pkey], f"w{w}bb{n}")
                so(lambda e, nn=n, pp=pb: e.copy(
                    out=b_bbc[:, nn * 2 * W:nn * 2 * W + W], in_=pp[:]),
                   [pkey], [f"bbc_{n}"], f"w{w}bbe{n}")
                to(lambda e, o=pb, nn=n, b=br: e.matmul(
                    out=o[:], lhsT=b_sel[:, nn * P:(nn + 1) * P], rhs=b[0:NST, W:2 * W],
                    start=True, stop=True),
                   ["b_sel", brk], [pkey], f"w{w}cb{n}")
                so(lambda e, nn=n, pp=pb: e.copy(
                    out=b_bbc[:, nn * 2 * W + W:nn * 2 * W + 2 * W], in_=pp[:]),
                   [pkey], [f"bbc_{n}"], f"w{w}cbe{n}")

            for m in range(MT):
                vo(lambda e, o=b_wm[m % 2], mm=m: e.tensor_mul(
                    out=o[:], in0=ws(b_delta, mm), in1=ws(b_uc, mm)),
                   [f"dl_{m}", f"uc_{m}"], [f"wm{m % 2}"], f"w{w}W{m}")
                for n in range(NST):
                    pp = n % 2
                    idx = m * NST + n
                    so(lambda e, o=b_da[pp], mm=m, ii=idx: e.activation(
                        out=o[:], in_=ws(b_delta, mm), func=AF.Exp,
                        scale=b_aneg[:, ii:ii + 1]),
                       [f"dl_{m}", "b_aneg"], [f"da{pp}"], f"w{w}dA{m}_{n}")
                    vo(lambda e, o=b_wb[pp], nn=n, mm=m: e.tensor_mul(
                        out=o[:], in0=b_wm[mm % 2][:],
                        in1=b_bbc[:, nn * 2 * W:nn * 2 * W + W]),
                       [f"wm{m % 2}", f"bbc_{n}"], [f"wbp{pp}"], f"w{w}WB{m}_{n}")
                    vo(lambda e, ii=idx, p2=pp: e.tensor_tensor_scan(
                        out=b_hh[p2][:], data0=b_da[p2][:], data1=b_wb[p2][:],
                        initial=b_hst[:, ii:ii + 1], op0=OP.mult, op1=OP.add),
                       [f"da{pp}", f"wbp{pp}", f"hst{idx}"], [f"hh{pp}"], f"w{w}sc{m}_{n}")
                    go(lambda e, ii=idx, p2=pp: e.tensor_copy(
                        out=b_hst[:, ii:ii + 1], in_=b_hh[p2][:, W - 1:W]),
                       [f"hh{pp}"], [f"hst{idx}"], f"w{w}hc{m}_{n}")
                    if debug and w == 0 and m == 0 and n == 0:
                        dm("sync", lambda e: e.dma_start(out=dump_h["da00"][:], in_=b_da[0][:]),
                           ["da0"], ["dumped_da00"], "dump_da00")
                        dm("sync", lambda e: e.dma_start(out=dump_h["wb00"][:], in_=b_wb[0][:]),
                           ["wbp0"], ["dumped_wb00"], "dump_wb00")
                        dm("sync", lambda e: e.dma_start(out=dump_h["hh00"][:], in_=b_hh[0][:]),
                           ["hh0"], ["dumped_hh00"], "dump_hh00")
                    go(lambda e, o=b_tmp[pp], nn=n, p2=pp: e.tensor_mul(
                        out=o[:], in0=b_hh[p2][:],
                        in1=b_bbc[:, nn * 2 * W + W:nn * 2 * W + 2 * W]),
                       [f"hh{pp}", f"bbc_{n}"], [f"tmp{pp}"], f"w{w}P4_{m}_{n}")
                    to(lambda e, o=ps_bc[m % 2], p2=pp, st=(n == 0), sp=(n == NST - 1):
                       e.matmul(out=o[:], lhsT=b_ide[:], rhs=b_tmp[p2][:],
                                start=st, stop=sp),
                       ["b_ide", f"tmp{pp}"], [f"psbc{m % 2}"], f"w{w}P5_{m}_{n}")
                vo(lambda e, mm=m: e.tensor_scalar(
                    out=b_g[:], in0=ws(b_uc, mm), scalar1=bv["v_dp"][:, mm:mm + 1],
                    scalar2=None, op0=OP.mult),
                   [f"uc_{m}", "sb_v_dp"], ["b_g"], f"w{w}uD{m}")
                vo(lambda e, mm=m: e.tensor_add(out=b_g[:], in0=b_g[:],
                                                in1=ps_bc[mm % 2][:]),
                   [f"psbc{m % 2}", "b_g"], ["b_g"], f"w{w}g{m}")
                vo(lambda e, mm=m: e.tensor_mul(
                    out=ws(b_res, mm), in0=b_g[:], in1=ws(b_res, mm)),
                   ["b_g", f"res_{m}"], [f"res_{m}"], f"w{w}y3{m}")
                if m == 5 or m == 13:
                    drain_c(26)

            drain_c()
            if debug:
                dm("sync", lambda e, ww=w: e.dma_start(out=dump_h["ay3"][ww], in_=b_res[:]),
                   [f"res_{k}" for k in range(MT)], ["dumped_ay3"], f"d_ay3{w}")
            if "C" not in PHASES:
                continue
            for mo in range(DT):
                wbuf = b_wop[mo % 2]
                dm("sync", lambda e, o=wbuf, i=scr["k_op"][mo]: e.dma_start(out=o[:], in_=i),
                   [f"k_op_{mo}"], [f"wop{mo % 2}"], f"w{w}wop{mo}")
                pb = ps_c[mo % 2]
                pkey = f"psc{mo % 2}"
                for k in range(MT):
                    to(lambda e, o=pb, l=wbuf, kk=k, st=(k == 0), sp=(k == MT - 1):
                       e.matmul(out=o[:], lhsT=l[:, kk * P:(kk + 1) * P],
                                rhs=ws(b_res, kk), start=st, stop=sp),
                       [f"wop{mo % 2}", f"res_{k}"], [pkey], f"w{w}op{mo}_{k}")
                vo(lambda e, o=ws(b_o16, mo), pp=pb, xx=xb, mm=mo: e.tensor_add(
                    out=o, in0=pp[:], in1=ws(xx, mm)),
                   [pkey, f"{xkey}_{mo}"], [f"o16_{mo}"], f"w{w}oev{mo}")

            if debug:
                dm("sync", lambda e, ww=w: e.dma_start(out=dump_h["ao16"][ww], in_=b_o16[:]),
                   [f"o16_{k}" for k in range(DT)], ["dumped_ao16"], f"d_ao{w}")
            def mm_block(scr_key, src_buf, skey, out_cb, nK, tg):
                def piece(mo):
                    wbuf = b_wsm[mo % 2]
                    dm("sync", lambda e, o=wbuf, i=scr[scr_key][mo]:
                       e.dma_start(out=o[:], in_=i),
                       [f"{scr_key}_{mo}"], [f"wsm{mo % 2}"], f"{tg}w{mo}")
                    pb = ps_c[mo % 2]
                    pkey = f"psc{mo % 2}"
                    for k in range(nK):
                        to(lambda e, o=pb, l=wbuf, kk=k, st=(k == 0), sp=(k == nK - 1):
                           e.matmul(out=o[:], lhsT=l[:, kk * P:(kk + 1) * P],
                                    rhs=ws(src_buf, kk), start=st, stop=sp),
                           [f"wsm{mo % 2}", f"{skey}_{k}"], [pkey], f"{tg}m{mo}_{k}")
                    out_cb(mo, pb, pkey)
                for mo in range(DT):
                    pending_c.append(lambda mo=mo: piece(mo))

            def gelu_out(mo, pb, pkey, ww=w):
                so(lambda e, o=ws(b_p1, mo), pp=pb, mm=mo: e.activation(
                    out=o, in_=pp[:], func=AF.Gelu, bias=bv["v_fb1"][:, mm:mm + 1]),
                   [pkey, "sb_v_fb1"], [f"p1_{mo}"], f"w{ww}gel{mo}")
            mm_block("k_f1", b_o16, "o16", gelu_out, DT, f"w{w}f1")

            def z_out(mo, pb, pkey, ww=w):
                vo(lambda e, o=ws(b_p2, mo), pp=pb, mm=mo: e.scalar_tensor_tensor(
                    out=o, in0=pp[:], scalar=bv["v_fb2"][:, mm:mm + 1],
                    in1=ws(b_o16, mm), op0=OP.add, op1=OP.add),
                   [pkey, "sb_v_fb2", f"o16_{mo}"], [f"p2_{mo}"], f"w{ww}zev{mo}")
            mm_block("k_f2", b_p1, "p1", z_out, DT, f"w{w}f2")

            if debug and w == 0:
                def w0_dumps():
                    _w0dumps()
                pending_c.append(w0_dumps)
            if debug and w == 0:
                def _w0dumps():
                    _run_w0_dumps()
            if debug and w == 0:
                def _run_w0_dumps():
                    pass
            if debug and w == 0 and False:
                for nm, src_, rd in [("xn", b_xn, [f"xn_{k}" for k in range(DT)]),
                                    ("uc", b_uc, [f"uc_{k}" for k in range(MT)]),
                                    ("res", b_res, [f"res_{k}" for k in range(MT)]),
                                    ("delta", b_delta, [f"dl_{k}" for k in range(MT)]),
                                    ("dtr", b_dtr, ["b_dtr"]),
                                    ("brow", br, [brk]),
                                    ("bbc", b_bbc, ["bbc_0"]),
                                    ("y3", b_res, [f"res_{k}" for k in range(MT)]),
                                    ("o16", b_o16, [f"o16_{k}" for k in range(DT)]),
                                    ("zz", b_p2, [f"p2_{k}" for k in range(DT)])]:
                    ap = src_[:, 0:2 * W] if nm == "bbc" else src_[:]
                    dm("sync", lambda e, o=dump_h[nm], i=ap: e.dma_start(out=o[:], in_=i),
                       rd, [f"dumped_{nm}"], f"dump_{nm}")

            if debug:
                pending_c.append(lambda ww=w: dm(
                    "sync", lambda e: e.dma_start(out=dump_h["azz"][ww], in_=b_p2[:]),
                    [f"p2_{k}" for k in range(DT)], ["dumped_azz"], f"d_az{ww}"))
            pending_c.append(lambda ww=w: layer_norm(b_p2, "p2", b_o16, "o16", "g",
                                                     ww, f"w{ww}L1"))
            def xv_piece(xx=xb, xk=xkey, ww=w):
                for k in range(DT):
                    vo(lambda e, kk=k: e.tensor_add(
                        out=ws(b_p1, kk), in0=ws(b_o16, kk), in1=ws(xx, kk)),
                       [f"o16_{k}", f"{xk}_{k}"], [f"p1_{k}"], f"w{ww}xv{k}")
            pending_c.append(xv_piece)
            pending_c.append(lambda ww=w: layer_norm(b_p1, "p1", b_p2, "p2", None,
                                                     ww, f"w{ww}L2"))
            def copy_out(dst_buf, dkey, ww=w):
                def cb(mo, pb, pkey):
                    vo(lambda e, o=ws(dst_buf, mo), pp=pb: e.tensor_copy(
                        out=o, in_=pp[:]),
                       [pkey], [f"{dkey}_{mo}"], f"w{ww}ffe{dkey}{mo}")
                return cb
            mm_block("k_n0", b_p2, "p2", copy_out(b_p1, "p1"), DT, f"w{w}n0")
            mm_block("k_n1", b_p1, "p1", copy_out(b_o16, "o16"), DT, f"w{w}n1")

            def tanh_out(mo, pb, pkey, ww=w):
                so(lambda e, o=ws(b_p1, mo), pp=pb, mm=mo: e.activation(
                    out=o, in_=pp[:], func=AF.Tanh, bias=bv["v_nb"][:, mm:mm + 1]),
                   [pkey, "sb_v_nb"], [f"p1_{mo}"], f"w{ww}tanh{mo}")
            mm_block("k_n2", b_o16, "o16", tanh_out, DT, f"w{w}n2")
            def v2_piece(ww=w):
                for k in range(DT):
                    vo(lambda e, kk=k: e.tensor_add(
                        out=ws(b_o16, kk), in0=ws(b_p1, kk), in1=ws(b_p2, kk)),
                       [f"p1_{k}", f"p2_{k}"], [f"o16_{k}"], f"w{ww}v2{k}")
            pending_c.append(v2_piece)
            pending_c.append(lambda ww=w, tt=t0: layer_norm(
                b_o16, "o16", None, None, None, ww, f"w{ww}L3", out32_t0=tt))

        drain_c()
        if "C" in PHASES:
            fence_reads = ["out"] + (["dumped_" + nm for nm in dump_h] if debug else [])
        else:
            fence_reads = ["dumped_" + nm for nm in dump_h if ("dumped_" + nm) in R.last_w] if debug else []
            fence_reads = [k for k in fence_reads]
        R.op("sync", lambda e: None, fence_reads, [], tag="fence")

        R.schedule()
        sems = {nm: ctx.enter_context(nc.semaphore("s_" + nm)) for nm in R.sem_names}
        waited = {e: {} for e in ENGINES}
        with nc.Block() as block:
            @block.sync
            def _(e):
                R.emit_engine("sync", e, sems, waited)

            @block.tensor
            def _(e):
                R.emit_engine("tensor", e, sems, waited)

            @block.scalar
            def _(e):
                R.emit_engine("scalar", e, sems, waited)

            @block.vector
            def _(e):
                R.emit_engine("vector", e, sems, waited)

            @block.gpsimd
            def _(e):
                R.emit_engine("gpsimd", e, sems, waited)
    return nc


def host_prep(inputs):
    g = {}
    ipw = np.ascontiguousarray(np.asarray(inputs["L0_attn_in_proj_w"], np.float32).T)
    g["w_in"] = np.ascontiguousarray(
        ipw.reshape(DT, P, 2 * MT, P).transpose(2, 1, 0, 3).reshape(2 * MT, P, DT * P))
    opw = np.ascontiguousarray(np.asarray(inputs["L0_attn_out_proj_w"], np.float32).T)
    g["w_op"] = np.ascontiguousarray(
        opw.reshape(MT, P, DT, P).transpose(2, 1, 0, 3).reshape(DT, P, MT * P))
    for nm, wsrc in [("w_f1", np.asarray(inputs["L0_attn_ff_w1"], np.float32).T),
                     ("w_f2", np.asarray(inputs["L0_attn_ff_w2"], np.float32).T),
                     ("w_n0", np.asarray(inputs["L0_ffn_weight"], np.float32)),
                     ("w_n1", np.asarray(inputs["L0_ffn_aux_weight1"], np.float32)),
                     ("w_n2", np.asarray(inputs["L0_ffn_aux_weight2"], np.float32))]:
        wsrc = np.ascontiguousarray(wsrc)
        g[nm] = np.ascontiguousarray(
            wsrc.reshape(DT, P, DT, P).transpose(2, 1, 0, 3).reshape(DT, P, DM))
    g["w_xp"] = np.ascontiguousarray(
        np.asarray(inputs["L0_attn_x_proj_w"], np.float32).T.reshape(MT, P, 96))
    g["w_dt"] = np.ascontiguousarray(np.asarray(inputs["L0_attn_dt_w"], np.float32).T)
    g["v_nw"] = np.ascontiguousarray(
        np.asarray(inputs["L0_attn_norm_w"], np.float32).reshape(DT, P).T)
    g["v_cw"] = np.ascontiguousarray(
        np.asarray(inputs["L0_attn_conv_w"], np.float32)
        .reshape(MT, P, 4).transpose(1, 0, 2).reshape(P, MT * 4))
    for nm, src, k in [("v_cb", "L0_attn_conv_b", MT), ("v_dtb", "L0_attn_dt_b", MT),
                       ("v_dp", "L0_attn_Dp", MT)]:
        g[nm] = np.ascontiguousarray(np.asarray(inputs[src], np.float32).reshape(k, P).T)
    g["v_alog"] = np.ascontiguousarray(
        np.asarray(inputs["L0_attn_A_log"], np.float32)
        .reshape(MT, P, NST).transpose(1, 0, 2).reshape(P, MT * NST))
    for nm, src in [("v_fb1", "L0_attn_ff_b1"), ("v_fb2", "L0_attn_ff_b2"),
                    ("v_lw", "L0_attn_ff_ln_w"), ("v_lb", "L0_attn_ff_ln_b"),
                    ("v_nb", "L0_ffn_bias")]:
        g[nm] = np.ascontiguousarray(np.asarray(inputs[src], np.float32).reshape(DT, P).T)
    sel = np.zeros((NST, NST * P), np.float16)
    for n in range(NST):
        sel[n, n * P:(n + 1) * P] = 1.0
    g["c_sel"] = sel
    g["c_ide"] = np.eye(P, dtype=np.float16)
    return g


_cached = {}


def kernel(**inputs):
    glob = host_prep(inputs)
    x = np.asarray(inputs["L0_attn_x"], np.float32)
    B = x.shape[0]
    key = ("prog", DEBUG)
    if key not in _cached:
        _cached[key] = build_program(DEBUG)
    nc = _cached[key]
    in_maps = []
    for b in range(B):
        m = dict(glob)
        m["x"] = np.ascontiguousarray(x[b].T)
        in_maps.append(m)
    res = run_bass_kernel_spmd(nc, in_maps, core_ids=list(range(B)))
    out = np.stack([np.ascontiguousarray(r["out"].T) for r in res.results])
    kernel.last_results = res.results
    return out


# version 23
# speedup vs baseline: 1.1556x; 1.1556x over previous
import numpy as np
import concourse.bass as bass
import concourse.mybir as mybir
from concourse.bass_utils import run_bass_kernel_spmd

F32 = mybir.dt.float32
F16 = mybir.dt.float16
AF = mybir.ActivationFunctionType
OP = mybir.AluOpType

P = 128
T = 4096
W = 512
NWIN = T // W
DM = 1024
DI = 2048
NST = 16
DTR = 64
MT = DI // P
DT = DM // P

DEBUG = False
PARANOID = False
import os as _os
PHASES = _os.environ.get("KPHASES", "ASC")

ENGINES = ("sync", "tensor", "scalar", "vector", "gpsimd")


class Rec:

    def __init__(self):
        self.ops = []
        self.last_w = {}
        self.readers = {}

    def op(self, eng, fn, reads=(), writes=(), dma=False, tag=""):
        i = len(self.ops)
        deps = set()
        for b in reads:
            if b in self.last_w:
                deps.add(self.last_w[b])
        for b in writes:
            if b in self.last_w:
                deps.add(self.last_w[b])
            for r in self.readers.get(b, ()):
                deps.add(r)
        self.ops.append(dict(eng=eng, fn=fn, dma=dma, deps=deps, tag=tag,
                             flagged=False, sem=None, val=None, gate=None))
        for b in reads:
            self.readers.setdefault(b, []).append(i)
        for b in writes:
            self.last_w[b] = i
            self.readers[b] = []
        return i

    def schedule(self):
        ops = self.ops
        for o in ops:
            kept = set()
            for d in o["deps"]:
                p = ops[d]
                if p["dma"] or p["eng"] != o["eng"] or PARANOID:
                    kept.add(d)
            o["deps"] = kept
            for d in kept:
                ops[d]["flagged"] = True
        CAP = 4000
        NPOOL = 48
        cc = {e: 0 for e in ENGINES}
        dseq = {e: 0 for e in ENGINES}
        uses = {}
        self.sem_names = set()
        for o in ops:
            if o["dma"]:
                o["flagged"] = True
                q = o["eng"]
                pi = dseq[q] % NPOOL
                dseq[q] += 1
                key = (q, pi)
                uses[key] = uses.get(key, 0) + 1
                o["sem"], o["val"] = f"q_{q}_{pi}", 16 * uses[key]
                o["gate"] = (o["sem"], o["val"] - 16) if uses[key] > 1 else None
            elif o["flagged"]:
                v = cc[o["eng"]] + 1
                cc[o["eng"]] += 1
                ep, vin = divmod(v - 1, CAP)
                o["sem"], o["val"] = f"c_{o['eng']}{ep}", vin + 1
            if o["sem"]:
                self.sem_names.add(o["sem"])

    def emit_engine(self, eng_name, eng, sems, waited):
        wt = waited[eng_name]
        for o in self.ops:
            if o["eng"] != eng_name:
                continue
            tgt = {}
            if o.get("gate"):
                tgt[o["gate"][0]] = o["gate"][1]
            for d in o["deps"]:
                p = self.ops[d]
                if p["sem"] is None:
                    continue
                if (not p["dma"]) and p["eng"] == eng_name and not PARANOID:
                    continue
                if tgt.get(p["sem"], 0) < p["val"]:
                    tgt[p["sem"]] = p["val"]
            for s, v in sorted(tgt.items()):
                if wt.get(s, 0) < v:
                    eng.wait_ge(sems[s], v)
                    wt[s] = v
            ins = o["fn"](eng)
            if o["flagged"]:
                assert ins is not None, f"flagged op returned None: {o['tag']}"
                ins.then_inc(sems[o["sem"]], 16 if o["dma"] else 1)


def build_program(debug):
    nc = bass.Bass("TRN2", target_bir_lowering=False)
    d = {}

    def din(name, shape, dt=F32):
        d[name] = nc.dram_tensor(name, list(shape), dt, kind="ExternalInput").ap()
        return d[name]

    x_h = din("x", (DM, T))
    w_in_h = din("w_in", (2 * MT, P, DT * P))
    w_op_h = din("w_op", (DT, P, MT * P))
    w_f1_h = din("w_f1", (DT, P, DM))
    w_f2_h = din("w_f2", (DT, P, DM))
    w_n0_h = din("w_n0", (DT, P, DM))
    w_n1_h = din("w_n1", (DT, P, DM))
    w_n2_h = din("w_n2", (DT, P, DM))
    w_xp_h = din("w_xp", (MT, P, 96))
    w_dt_h = din("w_dt", (DTR, DI))
    vh = {}
    for nm, cols in [("v_nw", DT), ("v_cw", MT * 4), ("v_cb", MT), ("v_dtb", MT),
                     ("v_dp", MT), ("v_alog", MT * NST), ("v_fb1", DT), ("v_fb2", DT),
                     ("v_lw", DT), ("v_lb", DT), ("v_nb", DT)]:
        vh[nm] = din(nm, (P, cols))
    sel_h = din("c_sel", (NST, NST * P), F16)
    ide_h = din("c_ide", (P, P), F16)
    out_h = nc.dram_tensor("out", [DM, T], F32, kind="ExternalOutput").ap()

    scr = {}
    for nm, shape in [("k_in", (2 * MT, P, DT * P)), ("k_op", (DT, P, MT * P)),
                      ("k_f1", (DT, P, DM)), ("k_f2", (DT, P, DM)),
                      ("k_n0", (DT, P, DM)), ("k_n1", (DT, P, DM)),
                      ("k_n2", (DT, P, DM))]:
        scr[nm] = nc.dram_tensor(nm, list(shape), F16, kind="Internal").ap()

    dump_h = {}
    if debug:
        for nm, shape, dt in [("xn", (P, DT * W), F16), ("uc", (P, MT * W), F16),
                              ("res", (P, MT * W), F16), ("delta", (P, MT * W), F16),
                              ("dtr", (DTR, W), F16), ("brow", (NST, 2 * W), F16),
                              ("bbc", (P, 2 * W), F16), ("hh", (P, W), F16),
                              ("y3", (P, MT * W), F16), ("o16", (P, DT * W), F16),
                              ("zz", (P, DT * W), F16), ("xmid", (P, DT * W), F16),
                              ("ym0", (P, W), F16), ("da00", (P, W), F32),
                              ("wb00", (P, W), F16), ("hh00", (P, W), F16),
                              ("z2t", (P, DT * W), F16), ("v2", (P, DT * W), F16),
                              ("auc", (NWIN, P, MT * W), F16),
                              ("adelta", (NWIN, P, MT * W), F16),
                              ("ay3", (NWIN, P, MT * W), F16),
                              ("ao16", (NWIN, P, DT * W), F16),
                              ("azz", (NWIN, P, DT * W), F16),
                              ("axv", (NWIN, P, DT * W), F16),
                              ("ax", (NWIN, P, DT * W), F16),
                              ("arstd", (NWIN, P, W), F16),
                              ("axn", (NWIN, P, DT * W), F16),
                              ("tl", (NWIN, P, MT * 3), F16),
                              ("ue0", (NWIN, P, W + 3), F16)]:
            dump_h[nm] = nc.dram_tensor("dump_" + nm, list(shape), dt,
                                        kind="ExternalOutput").ap()

    from contextlib import ExitStack
    ctx = ExitStack()

    def sb(name, shape, dt):
        return ctx.enter_context(nc.sbuf_tensor(name, list(shape), dt))

    def psum(name, shape):
        return ctx.enter_context(nc.psum_tensor(name, list(shape), F32))

    with ctx:
        b_xp = sb("b_xp", (P, MT * 96), F16)
        b_dt = sb("b_dt", (DTR, DI), F16)
        bv = {nm: sb("b" + nm, (P, c), F32) for nm, c in
              [("v_nw", DT), ("v_cw", MT * 4), ("v_cb", MT), ("v_dtb", MT),
               ("v_dp", MT), ("v_alog", MT * NST), ("v_fb1", DT), ("v_fb2", DT),
               ("v_lw", DT), ("v_lb", DT), ("v_nb", DT)]}
        b_aneg = sb("b_aneg", (P, MT * NST), F32)
        b_sel = sb("b_sel", (NST, NST * P), F16)
        b_ide = sb("b_ide", (P, P), F16)
        b_ones = sb("b_ones", (P, P), F16)
        b_cst = sb("b_cst", (P, 2), F32)
        b_win = [sb(f"b_win{i}", (P, DT * P), F16) for i in range(2)]
        b_wop = [sb(f"b_wop{i}", (P, MT * P), F16) for i in range(2)]
        b_wsm = [sb(f"b_wsm{i}", (P, DM), F16) for i in range(2)]
        b_x = [sb(f"b_x{i}", (P, DT * W), F16) for i in range(2)]
        b_xstg = [sb(f"b_xstg{i}", (P, W), F32) for i in range(2)]
        b_pstg = [sb(f"b_pstg{i}", (P, W), F32) for i in range(2)]
        b_pstg16 = [sb(f"b_pstg16{i}", (P, W), F16) for i in range(2)]
        b_xn = sb("b_xn", (P, DT * W), F16)
        b_x2 = [sb("b_x20", (P, W), F16)] * 2
        b_rstd = sb("b_rstd", (P, W), F16)
        b_ue = [sb(f"b_ue{i}", (P, W + 3), F16) for i in range(2)]
        b_tail = sb("b_tail", (P, MT * 3), F16)
        b_uc = sb("b_uc", (P, MT * W), F16)
        b_res = sb("b_res", (P, MT * W), F16)
        b_delta = sb("b_delta", (P, MT * W), F16)
        b_dtr = sb("b_dtr", (DTR, W), F16)
        b_brow = [sb("b_brow0", (NST, 2 * W), F16)] * 2
        b_vc = [sb("b_vc0", (P, W), F32)] * 2
        b_e32 = [sb("b_e320", (P, W), F32)] * 2
        b_cv = [sb("b_cv0", (P, W), F16)] * 2
        b_ct = [sb("b_ct0", (P, W), F16)] * 2
        b_bbc = sb("b_bbc", (P, NST * 2 * W), F16)
        b_da = [sb(f"b_da{i}", (P, W), F32) for i in range(2)]
        b_wb = [sb(f"b_wb{i}", (P, W), F16) for i in range(2)]
        b_hh = [sb(f"b_hh{i}", (P, W), F16) for i in range(2)]
        b_tmp = [sb(f"b_tmp{i}", (P, W), F16) for i in range(2)]
        b_wm = [sb(f"b_wm{i}", (P, W), F16) for i in range(2)]
        b_g = sb("b_g", (P, W), F16)
        b_hst = sb("b_hst", (P, MT * NST), F32)
        b_o16 = sb("b_o16", (P, DT * W), F16)
        b_p1 = sb("b_p1", (P, DT * W), F16)
        b_p2 = sb("b_p2", (P, DT * W), F16)
        b_mu = sb("b_mu", (P, W), F16)
        b_var = sb("b_var", (P, W), F32)
        b_cln = sb("b_cln", (P, W), F32)
        b_rstdc = sb("b_rstdc", (P, W), F16)
        b_z2 = [sb(f"b_z2{i}", (P, W), F16) for i in range(2)]
        b_q = [sb(f"b_q{i}", (P, W), F16) for i in range(2)]
        b_o32 = [sb("b_o320", (P, W), F32)] * 2

        ps_in = [psum("ps_in0", (P, W)), psum("ps_in1", (P, W)),
                 psum("ps_in2", (P, W))]
        ps_xp1 = psum("ps_xp1", (P, W))
        ps_bc = [psum("ps_bc0", (P, W)), psum("ps_bc1", (P, W))]
        ps_c = [psum("ps_c0", (P, W)), psum("ps_c1", (P, W))]

        R = Rec()
        vo = lambda fn, r, w_, tg="": R.op("vector", fn, r, w_, tag=tg)
        so = lambda fn, r, w_, tg="": R.op("scalar", fn, r, w_, tag=tg)
        to = lambda fn, r, w_, tg="": R.op("tensor", fn, r, w_, tag=tg)
        go = lambda fn, r, w_, tg="": R.op("gpsimd", fn, r, w_, tag=tg)
        dm = lambda eng, fn, r, w_, tg="": R.op(eng, fn, r, w_, dma=True, tag=tg)

        def ws(buf, j, width=W):
            return buf[:, j * width:(j + 1) * width]

        for nm in bv:
            dm("sync", lambda e, o=bv[nm], i=vh[nm]: e.dma_start(out=o[:], in_=i[:]),
               [nm], ["sb_" + nm], tg="ld" + nm)
        dm("sync", lambda e: e.dma_start(out=b_sel[:], in_=sel_h[:]),
           ["c_sel"], ["b_sel"], tg="ldsel")
        dm("sync", lambda e: e.dma_start(out=b_ide[:], in_=ide_h[:]),
           ["c_ide"], ["b_ide"], tg="ldide")
        for k in range(MT):
            dm("sync", lambda e, kk=k: e.dma_start(
                out=b_xstg[0][:, 0:96], in_=w_xp_h[kk]),
               ["w_xp"], ["xstg0"], tg=f"ldxp{k}")
            vo(lambda e, kk=k: e.tensor_copy(
                out=b_xp[:, kk * 96:(kk + 1) * 96], in_=b_xstg[0][:, 0:96]),
               ["xstg0"], ["b_xp"], f"cxp{k}")
        for k in range(4):
            dm("sync", lambda e, kk=k: e.dma_start(
                out=b_xstg[1][0:DTR, :], in_=w_dt_h[:, kk * W:(kk + 1) * W]),
               ["w_dt"], ["xstg1"], tg=f"lddt{k}")
            vo(lambda e, kk=k: e.tensor_copy(
                out=b_dt[:, kk * W:(kk + 1) * W], in_=b_xstg[1][0:DTR, :]),
               ["xstg1"], ["b_dt"], f"cdt{k}")
        so(lambda e: e.activation(out=b_aneg[:], in_=bv["v_alog"][:], func=AF.Exp),
           ["sb_v_alog"], ["b_aneg"], "aexp")
        vo(lambda e: e.tensor_scalar(out=b_aneg[:], in0=b_aneg[:], scalar1=-1.0,
                                     scalar2=None, op0=OP.mult),
           ["b_aneg"], ["b_aneg"], "aneg")
        vo(lambda e: e.memset(b_ones[:], 1.0), [], ["b_ones"], "ones")
        vo(lambda e: e.memset(b_cst[:, 0:1], 1e-5), [], ["b_cst"], "cst0")
        vo(lambda e: e.memset(b_cst[:, 1:2], 1.0), [], ["b_cst"], "cst1")
        vo(lambda e: e.memset(b_hst[:], 0.0), [], [f"hst{i}" for i in range(MT * NST)], "hst0")
        vo(lambda e: e.memset(b_tail[:], 0.0), [], ["b_tail"], "tail0")

        for src, dst, key in [(w_in_h, scr["k_in"], "k_in"), (w_op_h, scr["k_op"], "k_op"),
                              (w_f1_h, scr["k_f1"], "k_f1"), (w_f2_h, scr["k_f2"], "k_f2"),
                              (w_n0_h, scr["k_n0"], "k_n0"), (w_n1_h, scr["k_n1"], "k_n1"),
                              (w_n2_h, scr["k_n2"], "k_n2")]:
            cols = src.shape[2]
            nh = cols // W
            for bi in range(src.shape[0]):
                for hi in range(nh):
                    j = (bi * nh + hi) % 2
                    stg = b_pstg[j]
                    f16 = b_pstg16[j]
                    dm("sync", lambda e, o=stg, i=src[bi][:, hi * W:(hi + 1) * W]:
                       e.dma_start(out=o[:], in_=i),
                       [key[2:]], [f"pstg{j}"], tg=f"prep{key}{bi}_{hi}l")
                    vo(lambda e, o=f16, i=stg: e.tensor_copy(out=o[:], in_=i[:]),
                       [f"pstg{j}"], [f"pstg16{j}"], f"prep{key}{bi}_{hi}c")
                    dm("sync", lambda e, o=dst[bi][:, hi * W:(hi + 1) * W], i=f16:
                       e.dma_start(out=o, in_=i[:]),
                       [f"pstg16{j}"], [f"{key}_{bi}"], tg=f"prep{key}{bi}_{hi}s")

        def layer_norm(src_buf, skey, dst_buf, dkey, gamma_key, w, tg, out32_t0=None):
            for k in range(DT):
                to(lambda e, s=src_buf, kk=k, st=(k == 0), sp=(k == DT - 1):
                   e.matmul(out=ps_c[0][:], lhsT=b_ones[:], rhs=ws(s, kk),
                            start=st, stop=sp),
                   ["b_ones", f"{skey}_{k}"], ["psc0"], f"{tg}s1_{k}")
            for k in range(DT):
                z2 = b_z2[k % 2]
                vo(lambda e, o=z2, s=src_buf, kk=k: e.tensor_mul(
                    out=o[:], in0=ws(s, kk), in1=ws(s, kk)),
                   [f"{skey}_{k}"], [f"z2_{k % 2}"], f"{tg}sq{k}")
                to(lambda e, z=z2, st=(k == 0), sp=(k == DT - 1):
                   e.matmul(out=ps_c[1][:], lhsT=b_ones[:], rhs=z[:], start=st, stop=sp),
                   ["b_ones", f"z2_{k % 2}"], ["psc1"], f"{tg}s2_{k}")
            so(lambda e: e.mul(out=b_mu[:], in_=ps_c[0][:], mul=1.0 / DM),
               ["psc0"], ["b_mu"], f"{tg}mu")
            vo(lambda e: e.tensor_mul(out=b_z2[0][:], in0=b_mu[:], in1=b_mu[:]),
               ["b_mu"], ["z2_0"], f"{tg}mus")
            vo(lambda e: e.scalar_tensor_tensor(
                out=b_var[:], in0=ps_c[1][:], scalar=1.0 / DM, in1=b_z2[0][:],
                op0=OP.mult, op1=OP.subtract),
               ["psc1", "z2_0"], ["b_var"], f"{tg}var")
            so(lambda e: e.activation(out=b_cln[:], in_=b_var[:], func=AF.Ln,
                                      bias=b_cst[:, 0:1]),
               ["b_var", "b_cst"], ["b_cln"], f"{tg}ln")
            so(lambda e: e.activation(out=b_rstdc[:], in_=b_cln[:], func=AF.Exp,
                                      scale=-0.5),
               ["b_cln"], ["b_rstdc"], f"{tg}ex")
            for k in range(DT):
                q = b_q[k % 2]
                vo(lambda e, o=q, s=src_buf, kk=k: e.tensor_sub(
                    out=o[:], in0=ws(s, kk), in1=b_mu[:]),
                   [f"{skey}_{k}", "b_mu"], [f"q_{k % 2}"], f"{tg}q{k}")
                if out32_t0 is not None:
                    o32 = b_o32[k % 2]
                    vo(lambda e, o=o32, i=q: e.tensor_mul(out=o[:], in0=i[:],
                                                          in1=b_rstdc[:]),
                       [f"q_{k % 2}", "b_rstdc"], ["o32"], f"{tg}o{k}")
                    dm("sync", lambda e, i=o32, kk=k, t0=out32_t0: e.dma_start(
                        out=out_h[kk * P:(kk + 1) * P, t0:t0 + W], in_=i[:]),
                       ["o32"], ["out"], f"{tg}st{k}")
                elif gamma_key is not None:
                    vo(lambda e, o=q, i=q: e.tensor_mul(out=o[:], in0=i[:],
                                                        in1=b_rstdc[:]),
                       [f"q_{k % 2}", "b_rstdc"], [f"q_{k % 2}"], f"{tg}qr{k}")
                    vo(lambda e, o_=dst_buf, i=q, kk=k: e.tensor_scalar(
                        out=ws(o_, kk), in0=i[:], scalar1=bv["v_lw"][:, kk:kk + 1],
                        scalar2=bv["v_lb"][:, kk:kk + 1], op0=OP.mult, op1=OP.add),
                       [f"q_{k % 2}", "sb_v_lw", "sb_v_lb"], [f"{dkey}_{k}"],
                       f"{tg}ap{k}")
                else:
                    vo(lambda e, o_=dst_buf, i=q, kk=k: e.tensor_mul(
                        out=ws(o_, kk), in0=i[:], in1=b_rstdc[:]),
                       [f"q_{k % 2}", "b_rstdc"], [f"{dkey}_{k}"], f"{tg}qr{k}")

        pending_c = []

        def drain_c(k=None):
            nonlocal_marker = None
            n = len(pending_c) if k is None else min(k, len(pending_c))
            for _ in range(n):
                pending_c.pop(0)()

        for w in range(NWIN):
            xb = b_x[w % 2]
            xkey = f"x{w % 2}"
            t0 = w * W
            wpar = w % 2

            for k in range(DT):
                stg = b_xstg[k % 2]
                dm("sync", lambda e, o=stg, i=x_h[k * P:(k + 1) * P, t0:t0 + W]:
                   e.dma_start(out=o[:], in_=i),
                   ["x"], [f"xstg{k % 2}"], f"w{w}ldx{k}")
                vo(lambda e, o=ws(xb, k), i=stg: e.tensor_copy(out=o, in_=i[:]),
                   [f"xstg{k % 2}"], [f"{xkey}_{k}"], f"w{w}cx{k}")
            for k in range(DT):
                x2 = b_x2[k % 2]
                vo(lambda e, o=x2, i=ws(xb, k): e.tensor_mul(out=o[:], in0=i, in1=i),
                   [f"{xkey}_{k}"], ["x2_0"], f"w{w}x2_{k}")
                to(lambda e, i=x2, st=(k == 0), sp=(k == DT - 1):
                   e.matmul(out=ps_xp1[:], lhsT=b_ones[:], rhs=i[:], start=st, stop=sp),
                   ["b_ones", "x2_0"], ["ps_xp1"], f"w{w}xs{k}")
            so(lambda e: e.activation(out=b_vc[0][:], in_=ps_xp1[:], func=AF.Ln,
                                      scale=1.0 / DM, bias=b_cst[:, 0:1]),
               ["ps_xp1", "b_cst"], ["vc0"], f"w{w}rln")
            so(lambda e: e.activation(out=b_rstd[:], in_=b_vc[0][:], func=AF.Exp,
                                      scale=-0.5),
               ["vc0"], ["b_rstd"], f"w{w}rex")
            for k in range(DT):
                vo(lambda e, o=ws(b_xn, k), i=ws(xb, k), kk=k:
                   e.scalar_tensor_tensor(out=o, in0=i, scalar=bv["v_nw"][:, kk:kk + 1],
                                          in1=b_rstd[:], op0=OP.mult, op1=OP.mult),
                   [f"{xkey}_{k}", "b_rstd", "sb_v_nw"], [f"xn_{k}"], f"w{w}xn{k}")

            if debug:
                dm("sync", lambda e, ww=w, xx=xb: e.dma_start(out=dump_h["ax"][ww], in_=xx[:]),
                   [f"{xkey}_{k}" for k in range(DT)], ["dumped_ax"], f"d_ax{w}")
                dm("sync", lambda e, ww=w: e.dma_start(out=dump_h["arstd"][ww], in_=b_rstd[:]),
                   ["b_rstd"], ["dumped_arstd"], f"d_ars{w}")
                dm("sync", lambda e, ww=w: e.dma_start(out=dump_h["axn"][ww], in_=b_xn[:]),
                   [f"xn_{k}" for k in range(DT)], ["dumped_axn"], f"d_axn{w}")
            for m in range(2 * MT):
                wbuf = b_win[m % 2]
                dm("sync", lambda e, o=wbuf, i=scr["k_in"][m]: e.dma_start(out=o[:], in_=i),
                   [f"k_in_{m}"], [f"win{m % 2}"], f"w{w}win{m}")
                pb = ps_in[m % 3]
                pkey = f"psin{m % 3}"
                for k in range(DT):
                    to(lambda e, o=pb, l=wbuf, kk=k, st=(k == 0), sp=(k == DT - 1):
                       e.matmul(out=o[:], lhsT=l[:, kk * P:(kk + 1) * P],
                                rhs=ws(b_xn, kk), start=st, stop=sp),
                       [f"win{m % 2}", f"xn_{k}"], [pkey], f"w{w}ip{m}_{k}")
                if m < MT:
                    ue = b_ue[m % 2]
                    uek = f"ue{m % 2}"
                    vo(lambda e, o=ue, mm=m: e.tensor_copy(
                        out=o[:, 0:3], in_=b_tail[:, mm * 3:mm * 3 + 3]),
                       ["b_tail"], [uek], f"w{w}uh{m}")
                    vo(lambda e, o=ue, pp=pb: e.tensor_copy(out=o[:, 3:3 + W], in_=pp[:]),
                       [pkey], [uek], f"w{w}ue{m}")
                    go(lambda e, i=ue, mm=m: e.tensor_copy(
                        out=b_tail[:, mm * 3:mm * 3 + 3], in_=i[:, W:W + 3]),
                       [uek], ["b_tail"], f"w{w}ut{m}")
                    if debug and m == 0:
                        dm("gpsimd", lambda e, ww=w, i=ue: e.dma_start(
                            out=dump_h["ue0"][ww], in_=i[:]),
                           [uek], ["dumped_ue0"], f"d_ue{w}")
                    cv = b_cv[m % 2]
                    vo(lambda e, o=cv, i=ue, mm=m: e.tensor_scalar(
                        out=o[:], in0=i[:, 0:W], scalar1=bv["v_cw"][:, mm * 4:mm * 4 + 1],
                        scalar2=bv["v_cb"][:, mm:mm + 1], op0=OP.mult, op1=OP.add),
                       [uek, "sb_v_cw", "sb_v_cb"], ["cv0"], f"w{w}c0_{m}")
                    for kk in range(1, 4):
                        ct = b_ct[m % 2]
                        vo(lambda e, o=ct, i=ue, mm=m, k2=kk: e.tensor_scalar(
                            out=o[:], in0=i[:, k2:k2 + W],
                            scalar1=bv["v_cw"][:, mm * 4 + k2:mm * 4 + k2 + 1],
                            scalar2=None, op0=OP.mult),
                           [uek, "sb_v_cw"], ["ct0"], f"w{w}c{kk}_{m}")
                        vo(lambda e, o=cv, i=ct: e.tensor_add(out=o[:], in0=o[:],
                                                              in1=i[:]),
                           ["ct0", "cv0"], ["cv0"],
                           f"w{w}ca{kk}_{m}")
                    so(lambda e, o=ws(b_uc, m), i=cv: e.activation(
                        out=o, in_=i[:], func=AF.Silu),
                       ["cv0"], [f"uc_{m}"], f"w{w}silu{m}")
                else:
                    so(lambda e, o=ws(b_res, m - MT), pp=pb: e.activation(
                        out=o, in_=pp[:], func=AF.Silu),
                       [pkey], [f"res_{m - MT}"], f"w{w}rsilu{m - MT}")

            if debug:
                dm("gpsimd", lambda e, ww=w: e.dma_start(out=dump_h["tl"][ww], in_=b_tail[:]),
                   ["b_tail"], ["dumped_tl"], f"d_tl{w}")
            for (c0, c1, pt, pkey) in [(0, 64, ps_xp1, "ps_xp1"),
                                       (64, 80, ps_in[0], "psin0"),
                                       (80, 96, ps_in[1], "psin1")]:
                for k in range(MT):
                    to(lambda e, o=pt, kk=k, a=c0, b=c1, st=(k == 0), sp=(k == MT - 1):
                       e.matmul(out=o[0:b - a, :], lhsT=b_xp[:, kk * 96 + a:kk * 96 + b],
                                rhs=ws(b_uc, kk), start=st, stop=sp),
                       ["b_xp", f"uc_{k}"], [pkey], f"w{w}xp{c0}_{k}")
            vo(lambda e: e.tensor_copy(out=b_dtr[:], in_=ps_xp1[0:DTR, :]),
               ["ps_xp1"], ["b_dtr"], f"w{w}dtr")
            br = b_brow[0]
            brk = "brow0"
            vo(lambda e, o=br: e.tensor_copy(out=o[0:NST, 0:W], in_=ps_in[0][0:NST, :]),
               ["psin0"], [brk], f"w{w}bev")
            vo(lambda e, o=br: e.tensor_copy(out=o[0:NST, W:2 * W], in_=ps_in[1][0:NST, :]),
               ["psin1"], [brk], f"w{w}cev")

            for m in range(MT):
                pb = ps_in[m % 3]
                pkey = f"psin{m % 3}"
                to(lambda e, o=pb, mm=m: e.matmul(
                    out=o[:], lhsT=b_dt[:, mm * P:(mm + 1) * P], rhs=b_dtr[:],
                    start=True, stop=True),
                   ["b_dt", "b_dtr"], [pkey], f"w{w}dt{m}")
                vc = b_vc[m % 2]
                vo(lambda e, o=vc, pp=pb, mm=m: e.tensor_scalar(
                    out=o[:], in0=pp[:], scalar1=bv["v_dtb"][:, mm:mm + 1],
                    scalar2=80.0, op0=OP.add, op1=OP.min),
                   [pkey, "sb_v_dtb"], ["vc0"], f"w{w}sp0_{m}")
                e32 = b_e32[m % 2]
                so(lambda e, o=e32, i=vc: e.activation(out=o[:], in_=i[:], func=AF.Exp),
                   ["vc0"], ["e320"], f"w{w}sp1_{m}")
                so(lambda e, o=ws(b_delta, m), i=e32: e.activation(
                    out=o, in_=i[:], func=AF.Ln, bias=b_cst[:, 1:2]),
                   ["e320", "b_cst"], [f"dl_{m}"], f"w{w}sp2_{m}")

            if debug:
                dm("sync", lambda e, ww=w: e.dma_start(out=dump_h["auc"][ww], in_=b_uc[:]),
                   [f"uc_{k}" for k in range(MT)], ["dumped_auc"], f"d_auc{w}")
                dm("sync", lambda e, ww=w: e.dma_start(out=dump_h["adelta"][ww], in_=b_delta[:]),
                   [f"dl_{k}" for k in range(MT)], ["dumped_adelta"], f"d_adl{w}")
            if "S" not in PHASES:
                continue
            for n in range(NST):
                pb = ps_bc[n % 2]
                pkey = f"psbc{n % 2}"
                to(lambda e, o=pb, nn=n, b=br: e.matmul(
                    out=o[:], lhsT=b_sel[:, nn * P:(nn + 1) * P], rhs=b[0:NST, 0:W],
                    start=True, stop=True),
                   ["b_sel", brk], [pkey], f"w{w}bb{n}")
                so(lambda e, nn=n, pp=pb: e.copy(
                    out=b_bbc[:, nn * 2 * W:nn * 2 * W + W], in_=pp[:]),
                   [pkey], [f"bbc_{n}"], f"w{w}bbe{n}")
                to(lambda e, o=pb, nn=n, b=br: e.matmul(
                    out=o[:], lhsT=b_sel[:, nn * P:(nn + 1) * P], rhs=b[0:NST, W:2 * W],
                    start=True, stop=True),
                   ["b_sel", brk], [pkey], f"w{w}cb{n}")
                so(lambda e, nn=n, pp=pb: e.copy(
                    out=b_bbc[:, nn * 2 * W + W:nn * 2 * W + 2 * W], in_=pp[:]),
                   [pkey], [f"bbc_{n}"], f"w{w}cbe{n}")

            for m in range(MT):
                vo(lambda e, o=b_wm[m % 2], mm=m: e.tensor_mul(
                    out=o[:], in0=ws(b_delta, mm), in1=ws(b_uc, mm)),
                   [f"dl_{m}", f"uc_{m}"], [f"wm{m % 2}"], f"w{w}W{m}")
                for n in range(NST):
                    pp = n % 2
                    idx = m * NST + n
                    so(lambda e, o=b_da[pp], mm=m, ii=idx: e.activation(
                        out=o[:], in_=ws(b_delta, mm), func=AF.Exp,
                        scale=b_aneg[:, ii:ii + 1]),
                       [f"dl_{m}", "b_aneg"], [f"da{pp}"], f"w{w}dA{m}_{n}")
                    vo(lambda e, o=b_wb[pp], nn=n, mm=m: e.tensor_mul(
                        out=o[:], in0=b_wm[mm % 2][:],
                        in1=b_bbc[:, nn * 2 * W:nn * 2 * W + W]),
                       [f"wm{m % 2}", f"bbc_{n}"], [f"wbp{pp}"], f"w{w}WB{m}_{n}")
                    vo(lambda e, ii=idx, p2=pp: e.tensor_tensor_scan(
                        out=b_hh[p2][:], data0=b_da[p2][:], data1=b_wb[p2][:],
                        initial=b_hst[:, ii:ii + 1], op0=OP.mult, op1=OP.add),
                       [f"da{pp}", f"wbp{pp}", f"hst{idx}"], [f"hh{pp}"], f"w{w}sc{m}_{n}")
                    go(lambda e, ii=idx, p2=pp: e.tensor_copy(
                        out=b_hst[:, ii:ii + 1], in_=b_hh[p2][:, W - 1:W]),
                       [f"hh{pp}"], [f"hst{idx}"], f"w{w}hc{m}_{n}")
                    if debug and w == 0 and m == 0 and n == 0:
                        dm("sync", lambda e: e.dma_start(out=dump_h["da00"][:], in_=b_da[0][:]),
                           ["da0"], ["dumped_da00"], "dump_da00")
                        dm("sync", lambda e: e.dma_start(out=dump_h["wb00"][:], in_=b_wb[0][:]),
                           ["wbp0"], ["dumped_wb00"], "dump_wb00")
                        dm("sync", lambda e: e.dma_start(out=dump_h["hh00"][:], in_=b_hh[0][:]),
                           ["hh0"], ["dumped_hh00"], "dump_hh00")
                    vo(lambda e, o=b_tmp[pp], nn=n, p2=pp: e.tensor_mul(
                        out=o[:], in0=b_hh[p2][:],
                        in1=b_bbc[:, nn * 2 * W + W:nn * 2 * W + 2 * W]),
                       [f"hh{pp}", f"bbc_{n}"], [f"tmp{pp}"], f"w{w}P4_{m}_{n}")
                    to(lambda e, o=ps_bc[m % 2], p2=pp, st=(n == 0), sp=(n == NST - 1):
                       e.matmul(out=o[:], lhsT=b_ide[:], rhs=b_tmp[p2][:],
                                start=st, stop=sp),
                       ["b_ide", f"tmp{pp}"], [f"psbc{m % 2}"], f"w{w}P5_{m}_{n}")
                vo(lambda e, mm=m: e.tensor_scalar(
                    out=b_g[:], in0=ws(b_uc, mm), scalar1=bv["v_dp"][:, mm:mm + 1],
                    scalar2=None, op0=OP.mult),
                   [f"uc_{m}", "sb_v_dp"], ["b_g"], f"w{w}uD{m}")
                vo(lambda e, mm=m: e.tensor_add(out=b_g[:], in0=b_g[:],
                                                in1=ps_bc[mm % 2][:]),
                   [f"psbc{m % 2}", "b_g"], ["b_g"], f"w{w}g{m}")
                vo(lambda e, mm=m: e.tensor_mul(
                    out=ws(b_res, mm), in0=b_g[:], in1=ws(b_res, mm)),
                   ["b_g", f"res_{m}"], [f"res_{m}"], f"w{w}y3{m}")
                if m == 5 or m == 13:
                    drain_c(26)

            drain_c()
            if debug:
                dm("sync", lambda e, ww=w: e.dma_start(out=dump_h["ay3"][ww], in_=b_res[:]),
                   [f"res_{k}" for k in range(MT)], ["dumped_ay3"], f"d_ay3{w}")
            if "C" not in PHASES:
                continue
            for mo in range(DT):
                wbuf = b_wop[mo % 2]
                dm("sync", lambda e, o=wbuf, i=scr["k_op"][mo]: e.dma_start(out=o[:], in_=i),
                   [f"k_op_{mo}"], [f"wop{mo % 2}"], f"w{w}wop{mo}")
                pb = ps_c[mo % 2]
                pkey = f"psc{mo % 2}"
                for k in range(MT):
                    to(lambda e, o=pb, l=wbuf, kk=k, st=(k == 0), sp=(k == MT - 1):
                       e.matmul(out=o[:], lhsT=l[:, kk * P:(kk + 1) * P],
                                rhs=ws(b_res, kk), start=st, stop=sp),
                       [f"wop{mo % 2}", f"res_{k}"], [pkey], f"w{w}op{mo}_{k}")
                vo(lambda e, o=ws(b_o16, mo), pp=pb, xx=xb, mm=mo: e.tensor_add(
                    out=o, in0=pp[:], in1=ws(xx, mm)),
                   [pkey, f"{xkey}_{mo}"], [f"o16_{mo}"], f"w{w}oev{mo}")

            if debug:
                dm("sync", lambda e, ww=w: e.dma_start(out=dump_h["ao16"][ww], in_=b_o16[:]),
                   [f"o16_{k}" for k in range(DT)], ["dumped_ao16"], f"d_ao{w}")
            def mm_block(scr_key, src_buf, skey, out_cb, nK, tg):
                def piece(mo):
                    wbuf = b_wsm[mo % 2]
                    dm("sync", lambda e, o=wbuf, i=scr[scr_key][mo]:
                       e.dma_start(out=o[:], in_=i),
                       [f"{scr_key}_{mo}"], [f"wsm{mo % 2}"], f"{tg}w{mo}")
                    pb = ps_c[mo % 2]
                    pkey = f"psc{mo % 2}"
                    for k in range(nK):
                        to(lambda e, o=pb, l=wbuf, kk=k, st=(k == 0), sp=(k == nK - 1):
                           e.matmul(out=o[:], lhsT=l[:, kk * P:(kk + 1) * P],
                                    rhs=ws(src_buf, kk), start=st, stop=sp),
                           [f"wsm{mo % 2}", f"{skey}_{k}"], [pkey], f"{tg}m{mo}_{k}")
                    out_cb(mo, pb, pkey)
                for mo in range(DT):
                    pending_c.append(lambda mo=mo: piece(mo))

            def gelu_out(mo, pb, pkey, ww=w):
                so(lambda e, o=ws(b_p1, mo), pp=pb, mm=mo: e.activation(
                    out=o, in_=pp[:], func=AF.Gelu, bias=bv["v_fb1"][:, mm:mm + 1]),
                   [pkey, "sb_v_fb1"], [f"p1_{mo}"], f"w{ww}gel{mo}")
            mm_block("k_f1", b_o16, "o16", gelu_out, DT, f"w{w}f1")

            def z_out(mo, pb, pkey, ww=w):
                vo(lambda e, o=ws(b_p2, mo), pp=pb, mm=mo: e.scalar_tensor_tensor(
                    out=o, in0=pp[:], scalar=bv["v_fb2"][:, mm:mm + 1],
                    in1=ws(b_o16, mm), op0=OP.add, op1=OP.add),
                   [pkey, "sb_v_fb2", f"o16_{mo}"], [f"p2_{mo}"], f"w{ww}zev{mo}")
            mm_block("k_f2", b_p1, "p1", z_out, DT, f"w{w}f2")

            if debug and w == 0:
                def w0_dumps():
                    _w0dumps()
                pending_c.append(w0_dumps)
            if debug and w == 0:
                def _w0dumps():
                    _run_w0_dumps()
            if debug and w == 0:
                def _run_w0_dumps():
                    pass
            if debug and w == 0 and False:
                for nm, src_, rd in [("xn", b_xn, [f"xn_{k}" for k in range(DT)]),
                                    ("uc", b_uc, [f"uc_{k}" for k in range(MT)]),
                                    ("res", b_res, [f"res_{k}" for k in range(MT)]),
                                    ("delta", b_delta, [f"dl_{k}" for k in range(MT)]),
                                    ("dtr", b_dtr, ["b_dtr"]),
                                    ("brow", br, [brk]),
                                    ("bbc", b_bbc, ["bbc_0"]),
                                    ("y3", b_res, [f"res_{k}" for k in range(MT)]),
                                    ("o16", b_o16, [f"o16_{k}" for k in range(DT)]),
                                    ("zz", b_p2, [f"p2_{k}" for k in range(DT)])]:
                    ap = src_[:, 0:2 * W] if nm == "bbc" else src_[:]
                    dm("sync", lambda e, o=dump_h[nm], i=ap: e.dma_start(out=o[:], in_=i),
                       rd, [f"dumped_{nm}"], f"dump_{nm}")

            if debug:
                pending_c.append(lambda ww=w: dm(
                    "sync", lambda e: e.dma_start(out=dump_h["azz"][ww], in_=b_p2[:]),
                    [f"p2_{k}" for k in range(DT)], ["dumped_azz"], f"d_az{ww}"))
            pending_c.append(lambda ww=w: layer_norm(b_p2, "p2", b_o16, "o16", "g",
                                                     ww, f"w{ww}L1"))
            def xv_piece(xx=xb, xk=xkey, ww=w):
                for k in range(DT):
                    vo(lambda e, kk=k: e.tensor_add(
                        out=ws(b_p1, kk), in0=ws(b_o16, kk), in1=ws(xx, kk)),
                       [f"o16_{k}", f"{xk}_{k}"], [f"p1_{k}"], f"w{ww}xv{k}")
            pending_c.append(xv_piece)
            pending_c.append(lambda ww=w: layer_norm(b_p1, "p1", b_p2, "p2", None,
                                                     ww, f"w{ww}L2"))
            def copy_out(dst_buf, dkey, ww=w):
                def cb(mo, pb, pkey):
                    vo(lambda e, o=ws(dst_buf, mo), pp=pb: e.tensor_copy(
                        out=o, in_=pp[:]),
                       [pkey], [f"{dkey}_{mo}"], f"w{ww}ffe{dkey}{mo}")
                return cb
            mm_block("k_n0", b_p2, "p2", copy_out(b_p1, "p1"), DT, f"w{w}n0")
            mm_block("k_n1", b_p1, "p1", copy_out(b_o16, "o16"), DT, f"w{w}n1")

            def tanh_out(mo, pb, pkey, ww=w):
                so(lambda e, o=ws(b_p1, mo), pp=pb, mm=mo: e.activation(
                    out=o, in_=pp[:], func=AF.Tanh, bias=bv["v_nb"][:, mm:mm + 1]),
                   [pkey, "sb_v_nb"], [f"p1_{mo}"], f"w{ww}tanh{mo}")
            mm_block("k_n2", b_o16, "o16", tanh_out, DT, f"w{w}n2")
            def v2_piece(ww=w):
                for k in range(DT):
                    vo(lambda e, kk=k: e.tensor_add(
                        out=ws(b_o16, kk), in0=ws(b_p1, kk), in1=ws(b_p2, kk)),
                       [f"p1_{k}", f"p2_{k}"], [f"o16_{k}"], f"w{ww}v2{k}")
            pending_c.append(v2_piece)
            pending_c.append(lambda ww=w, tt=t0: layer_norm(
                b_o16, "o16", None, None, None, ww, f"w{ww}L3", out32_t0=tt))

        drain_c()
        if "C" in PHASES:
            fence_reads = ["out"] + (["dumped_" + nm for nm in dump_h] if debug else [])
        else:
            fence_reads = ["dumped_" + nm for nm in dump_h if ("dumped_" + nm) in R.last_w] if debug else []
            fence_reads = [k for k in fence_reads]
        R.op("sync", lambda e: None, fence_reads, [], tag="fence")

        R.schedule()
        sems = {nm: ctx.enter_context(nc.semaphore("s_" + nm)) for nm in R.sem_names}
        waited = {e: {} for e in ENGINES}
        with nc.Block() as block:
            @block.sync
            def _(e):
                R.emit_engine("sync", e, sems, waited)

            @block.tensor
            def _(e):
                R.emit_engine("tensor", e, sems, waited)

            @block.scalar
            def _(e):
                R.emit_engine("scalar", e, sems, waited)

            @block.vector
            def _(e):
                R.emit_engine("vector", e, sems, waited)

            @block.gpsimd
            def _(e):
                R.emit_engine("gpsimd", e, sems, waited)
    return nc


def host_prep(inputs):
    g = {}
    ipw = np.ascontiguousarray(np.asarray(inputs["L0_attn_in_proj_w"], np.float32).T)
    g["w_in"] = np.ascontiguousarray(
        ipw.reshape(DT, P, 2 * MT, P).transpose(2, 1, 0, 3).reshape(2 * MT, P, DT * P))
    opw = np.ascontiguousarray(np.asarray(inputs["L0_attn_out_proj_w"], np.float32).T)
    g["w_op"] = np.ascontiguousarray(
        opw.reshape(MT, P, DT, P).transpose(2, 1, 0, 3).reshape(DT, P, MT * P))
    for nm, wsrc in [("w_f1", np.asarray(inputs["L0_attn_ff_w1"], np.float32).T),
                     ("w_f2", np.asarray(inputs["L0_attn_ff_w2"], np.float32).T),
                     ("w_n0", np.asarray(inputs["L0_ffn_weight"], np.float32)),
                     ("w_n1", np.asarray(inputs["L0_ffn_aux_weight1"], np.float32)),
                     ("w_n2", np.asarray(inputs["L0_ffn_aux_weight2"], np.float32))]:
        wsrc = np.ascontiguousarray(wsrc)
        g[nm] = np.ascontiguousarray(
            wsrc.reshape(DT, P, DT, P).transpose(2, 1, 0, 3).reshape(DT, P, DM))
    g["w_xp"] = np.ascontiguousarray(
        np.asarray(inputs["L0_attn_x_proj_w"], np.float32).T.reshape(MT, P, 96))
    g["w_dt"] = np.ascontiguousarray(np.asarray(inputs["L0_attn_dt_w"], np.float32).T)
    g["v_nw"] = np.ascontiguousarray(
        np.asarray(inputs["L0_attn_norm_w"], np.float32).reshape(DT, P).T)
    g["v_cw"] = np.ascontiguousarray(
        np.asarray(inputs["L0_attn_conv_w"], np.float32)
        .reshape(MT, P, 4).transpose(1, 0, 2).reshape(P, MT * 4))
    for nm, src, k in [("v_cb", "L0_attn_conv_b", MT), ("v_dtb", "L0_attn_dt_b", MT),
                       ("v_dp", "L0_attn_Dp", MT)]:
        g[nm] = np.ascontiguousarray(np.asarray(inputs[src], np.float32).reshape(k, P).T)
    g["v_alog"] = np.ascontiguousarray(
        np.asarray(inputs["L0_attn_A_log"], np.float32)
        .reshape(MT, P, NST).transpose(1, 0, 2).reshape(P, MT * NST))
    for nm, src in [("v_fb1", "L0_attn_ff_b1"), ("v_fb2", "L0_attn_ff_b2"),
                    ("v_lw", "L0_attn_ff_ln_w"), ("v_lb", "L0_attn_ff_ln_b"),
                    ("v_nb", "L0_ffn_bias")]:
        g[nm] = np.ascontiguousarray(np.asarray(inputs[src], np.float32).reshape(DT, P).T)
    sel = np.zeros((NST, NST * P), np.float16)
    for n in range(NST):
        sel[n, n * P:(n + 1) * P] = 1.0
    g["c_sel"] = sel
    g["c_ide"] = np.eye(P, dtype=np.float16)
    return g


_cached = {}


def kernel(**inputs):
    glob = host_prep(inputs)
    x = np.asarray(inputs["L0_attn_x"], np.float32)
    B = x.shape[0]
    key = ("prog", DEBUG)
    if key not in _cached:
        _cached[key] = build_program(DEBUG)
    nc = _cached[key]
    in_maps = []
    for b in range(B):
        m = dict(glob)
        m["x"] = np.ascontiguousarray(x[b].T)
        in_maps.append(m)
    res = run_bass_kernel_spmd(nc, in_maps, core_ids=list(range(B)))
    out = np.stack([np.ascontiguousarray(r["out"].T) for r in res.results])
    kernel.last_results = res.results
    return out
